# version 01 (final)
# baseline (speedup 1.0000x reference)
import numpy as np
from contextlib import ExitStack
import concourse.bass as bass
import concourse.mybir as mybir
from concourse.bass_utils import run_bass_kernel_spmd

F32 = mybir.dt.float32
BF16 = mybir.dt.bfloat16
AF = mybir.ActivationFunctionType
ALU = mybir.AluOpType
AX = mybir.AxisListType

DM, L, H, DH, CH, NCH, NP = 1024, 2048, 16, 64, 64, 32, 8
E, FF = 8, 2048
BLK, NBLK = 256, 8

ENGINES = ("q", "g", "t", "v", "s")


class KB:

    def __init__(self, nc):
        self.nc = nc
        self.sems = {}
        self.cnt = {}
        self.cur = None
        self.obj = None
        self.strict = False

    def mksem(self, ctx, *names):
        for n in names:
            self.sems[n] = ctx.enter_context(self.nc.semaphore(n))

    def begin(self, eng, obj):
        self.cur = eng
        self.obj = obj
        for k in list(self.cnt):
            self.cnt[k] = 0

    def op(self, eng, fname, *args, inc=(), **kw):
        if self.strict:
            for f in ENGINES:
                if f != eng:
                    self.wait(eng, f, self.snap(f))
            if not any(s == eng for s, _ in inc) and fname != "dma_start":
                inc = list(inc) + [(eng, 1)]
        for sem, n in inc:
            self.cnt[sem] = self.cnt.get(sem, 0) + n
        if eng == self.cur:
            inst = getattr(self.obj, fname)(*args, **kw)
            for sem, n in inc:
                inst.then_inc(self.sems[sem], n)
            return inst
        return None

    def wait(self, eng, sem, val):
        if val <= 0:
            return
        if eng == self.cur:
            self.obj.wait_ge(self.sems[sem], val)

    def snap(self, sem):
        return self.cnt.get(sem, 0)


class Arena:
    def __init__(self, tensor, cols):
        self.t = tensor
        self.cols = cols
        self.off = 0

    def alloc(self, n):
        assert self.off + n <= self.cols, f"arena overflow {self.off}+{n}>{self.cols}"
        o = self.off
        self.off += n
        return o

    def tile(self, n):
        o = self.alloc(n)
        return self.t[:, o:o + n]


def host_inputs(x_core, ins):
    f32 = np.float32
    d = {}
    d["xT"] = np.ascontiguousarray(np.asarray(x_core, f32).T)
    for nm, W in (("wk", ins["L0_attn_Wk"]), ("wq", ins["L0_attn_Wq"]),
                  ("wv", ins["L0_attn_Wv"]), ("wout", ins["L0_attn_Wout"])):
        Wh = np.asarray(W, f32).reshape(8, 128, 8, 128).transpose(3, 2, 0, 1).reshape(128, 8192)
        d[nm] = np.ascontiguousarray(Wh)
    for nm, b in (("bk", ins["L0_attn_bk"]), ("bq", ins["L0_attn_bq"]),
                  ("bv", ins["L0_attn_bv"]), ("bout", ins["L0_attn_bout"])):
        d[nm] = np.ascontiguousarray(np.asarray(b, f32).reshape(8, 128).T)
    wb = np.asarray(ins["L0_attn_Wbeta"], f32)
    d["wbeta"] = np.ascontiguousarray(wb.reshape(16, 8, 128).transpose(2, 1, 0).reshape(128, 128))
    d["bbeta"] = np.ascontiguousarray(np.asarray(ins["L0_attn_bbeta"], f32).reshape(16, 1))
    for nm, cw in (("convk", ins["L0_attn_conv_k"]), ("convq", ins["L0_attn_conv_q"]),
                   ("convv", ins["L0_attn_conv_v"])):
        d[nm] = np.ascontiguousarray(np.asarray(cw, f32).reshape(8, 128, 4).transpose(1, 0, 2).reshape(128, 32))
    d["rmsw"] = np.ascontiguousarray(np.tile(np.asarray(ins["L0_attn_rms_w"], f32), 8)[None, :])
    wr = np.asarray(ins["L0_ffn_Wr"], f32)
    d["wr"] = np.ascontiguousarray(wr.reshape(8, 8, 128).transpose(2, 1, 0).reshape(128, 64))
    W1 = np.asarray(ins["L0_ffn_W1"], f32)
    d["w1s"] = np.ascontiguousarray(
        W1.reshape(8, 32, 128, 8, 128).transpose(0, 1, 4, 3, 2).reshape(8 * 32 * 128, 1024))
    W2 = np.asarray(ins["L0_ffn_W2"], f32)
    d["w2s"] = np.ascontiguousarray(W2.transpose(0, 2, 1).reshape(8 * 16 * 128, 1024))
    e8 = np.zeros((8, 8, 128), f32)
    for e in range(8):
        e8[e, e, :] = 1.0
    d["e8"] = np.ascontiguousarray(e8.reshape(8, 1024))
    d["ident"] = np.eye(128, dtype=f32)
    d["epsv"] = np.ascontiguousarray(
        np.tile(np.array([[0.0, 1.0, -1.0, 1e-5, 1e-6]], f32), (128, 1)))
    d["ones128"] = np.ones((128, 1), f32)
    d["ones1"] = np.ones((1, 128), f32)
    d["mskl"] = np.ascontiguousarray(np.tile(np.tril(np.ones((64, 64), f32), -1), (2, 1)))
    d["msku"] = np.ascontiguousarray(np.tile(np.triu(np.ones((64, 64), f32), 1), (2, 1)))
    e16 = np.zeros((8, 16, 128), f32)
    h16 = np.zeros((8, 128, 16), f32)
    for c in range(8):
        for pp in range(128):
            h = 2 * c + pp // 64
            e16[c, h, pp] = 1.0
            h16[c, pp, h] = 1.0
    d["e16"] = np.ascontiguousarray(e16.transpose(1, 0, 2).reshape(16, 8 * 128))
    d["e16n"] = np.ascontiguousarray(-d["e16"])
    d["h16"] = np.ascontiguousarray(h16.transpose(1, 0, 2).reshape(128, 128))
    idx0 = np.arange(1, CH + 1, dtype=np.float64)
    Z = lambda sh: np.zeros(sh, f32)
    a_num, a_den = Z((128, NCH)), Z((128, NCH))
    wi_num, wi_den = Z((128, NCH)), Z((128, NCH))
    c1_num, c1_den = Z((128, NCH)), Z((128, NCH))
    c2_num, c2_den = Z((128, NCH)), Z((128, NCH))
    tnumT, tden = Z((128, NCH * 64)), Z((1, NCH * 64))
    for i in range(NCH):
        ts = float(i * CH)
        idx = idx0 + ts
        cs = np.cumsum(idx)
        weighted = idx + idx.sum() - cs
        tri0 = (1 + ts) * ts / 2
        a_num[:, i] = tri0
        a_den[:, i] = np.tile(tri0 + cs, 2)
        wi_num[:, i] = np.tile(weighted, 2)
        wi_den[:, i] = (1 + ts + CH) * (ts + CH) / 2
        c1_num[:, i] = ts * (1 + ts)
        c1_den[:, i] = (ts + CH) * (1 + ts + CH)
        c2_num[:, i] = CH * (2 * ts + 1 + CH)
        c2_den[:, i] = (ts + CH) * (1 + ts + CH)
        r = cs[:, None]
        j_idx = (cs - idx)[None, :]
        num = np.where(r >= j_idx, r - j_idx, 0.0)
        tnumT[:, i * 64:(i + 1) * 64] = np.tile(num.T, (2, 1))
        tden[0, i * 64:(i + 1) * 64] = tri0 + cs
    for nm, v in (("a_num", a_num), ("a_den", a_den), ("wi_num", wi_num), ("wi_den", wi_den),
                  ("c1_num", c1_num), ("c1_den", c1_den), ("c2_num", c2_num), ("c2_den", c2_den),
                  ("tnumT", tnumT), ("tden", tden)):
        d[nm] = np.ascontiguousarray(v)
    return d


BLKP = 128
NBLKP = L // BLKP


class Prog:
    def __init__(self, nc, kb, DT, y, dbg, af, ab, pb, stage):
        self.nc, self.kb, self.T, self.y, self.dbg = nc, kb, DT, y, dbg
        self.af, self.ab, self.pb, self.stage = af, ab, pb, stage
        self.lay()

    def lay(self):
        ab, af = self.ab, self.af
        self.o_xn = ab.alloc(8 * L)
        self.o_ao = ab.alloc(8 * L)
        self.o_ident = ab.alloc(128)
        self.o_wbeta = ab.alloc(128)
        ab_phase = ab.off
        self.ab_phase = ab_phase
        self.o_wk = ab.alloc(8192)
        self.o_wq = ab.alloc(8192)
        self.o_wv = ab.alloc(8192)
        self.o_rk = ab.alloc(NP * 2 * BLKP)
        self.o_rq = ab.alloc(NP * 2 * BLKP)
        self.o_rv = ab.alloc(NP * 2 * BLKP)
        self.o_bns = ab.alloc(512)
        self.o_orng = ab.alloc(NP * 64)
        self.o_X = ab.alloc(3 * 512)
        self.o_pg = ab.alloc(1024)
        self.o_mt = ab.alloc(512)
        self.o_ktm = ab.alloc(512)
        self.o_gfm = ab.alloc(512)
        self.o_mpf = ab.alloc(512)
        self.o_mgt = ab.alloc(512)
        self.o_usb = ab.alloc(512)
        self.o_uwb = ab.alloc(512)
        self.o_rhs = self.o_usb
        self.o_stb = ab.alloc(1024)
        assert ab.off <= ab.cols, f"phase A bf overflow {ab.off}"
        self.o_act = ab_phase
        self.o_w2b = ab_phase + 16384
        self.o_w1sl = ab_phase + 32768
        assert ab_phase + 32768 + 4864 <= ab.cols

        self.o_sm = af.alloc(1024)
        self.o_ones = af.alloc(8)
        self.o_eps = af.alloc(8)
        self.o_ones1 = af.alloc(128)
        af_aonly = af.off
        self.o_beta = af.alloc(L)
        self.o_convw = af.alloc(96)
        self.o_bias = af.alloc(32)
        self.o_coef = af.alloc(6 * NCH)
        self.o_tmT = af.alloc(NCH * 64)
        self.o_msk = af.alloc(128)
        self.o_e16 = af.alloc(1024)
        self.o_e16n = af.alloc(1024)
        self.o_h16 = af.alloc(128)
        self.o_rmswb = af.alloc(512)
        af_phase = af.off
        self.af_phase = af_phase
        self.af_aonly = af_aonly
        self.o_xr = af.alloc(2 * L)
        self.o_sq = af.alloc(2 * 512)
        self.o_xsc = af.alloc(512)
        self.o_ost = af.alloc(2 * 512)
        self.o_st32 = af.alloc(1024)
        assert af.off <= af.cols, f"phase A f32 overflow {af.off}"
        self.o_acc = af_aonly
        self.o_eos = af_aonly + 8192
        self.o_lgt = af_aonly + 8704
        self.o_rwfm = af_aonly + 9472
        self.o_rwf = af_aonly + 11520
        self.o_e8 = af_aonly + 12544
        self.o_xr2 = af_aonly + 13568
        assert af_aonly + 14592 <= af.cols
        xr = self.o_xr
        self.o_scr = [xr + i * 132 for i in range(3)]
        self.o_co = [xr + 400 + i * 128 for i in range(3)]
        self.o_kk = xr + 800
        self.o_rn = xr + 936
        self.o_tail = xr + 4300
        self.o_coK = xr + 1200
        self.o_coQ = xr + 2224
        self.o_vs = xr + 3248

    def const_loads(self):
        c, T, af, ab = self.kb, self.T, self.af, self.ab
        f32_loads = [
            (af.t[:, self.o_convw + 0:self.o_convw + 32], T["convk"]),
            (af.t[:, self.o_convw + 32:self.o_convw + 64], T["convq"]),
            (af.t[:, self.o_convw + 64:self.o_convw + 96], T["convv"]),
            (af.t[:, self.o_bias + 0:self.o_bias + 8], T["bk"]),
            (af.t[:, self.o_bias + 8:self.o_bias + 16], T["bq"]),
            (af.t[:, self.o_bias + 16:self.o_bias + 24], T["bv"]),
            (af.t[:, self.o_bias + 24:self.o_bias + 32], T["bout"]),
            (af.t[:, self.o_msk + 0:self.o_msk + 64], T["mskl"]),
            (af.t[:, self.o_msk + 64:self.o_msk + 128], T["msku"]),
            (af.t[:, self.o_ones:self.o_ones + 1], T["ones128"]),
            (af.t[:, self.o_eps:self.o_eps + 5], T["epsv"]),
            (af.t[0:16, self.o_e16:self.o_e16 + 1024], T["e16"]),
            (af.t[0:16, self.o_e16n:self.o_e16n + 1024], T["e16n"]),
            (af.t[:, self.o_h16:self.o_h16 + 128], T["h16"]),
        ]
        for i, v in enumerate([0.0, 1.0, -1.0, 1e-5, 1e-6]):
            try:
                self.nc.const_aps.register(F32, v, af.t[:, self.o_eps + i:self.o_eps + i + 1])
            except Exception:
                pass
        for dst, src in f32_loads:
            c.op("q", "dma_start", dst, src, inc=[("qc", 16)])
        self.ones1 = af.t[0:1, self.o_ones1:self.o_ones1 + 128]
        c.op("q", "dma_start", self.ones1, T["ones1"], inc=[("qc", 16)])
        self.bbeta = af.t[0:16, self.o_ones + 4:self.o_ones + 5]
        c.op("q", "dma_start", self.bbeta, T["bbeta"], inc=[("qc", 16)])
        bf_loads = [
            (ab.t[:, self.o_wk:self.o_wk + 8192], T["wk"]),
            (ab.t[:, self.o_wq:self.o_wq + 8192], T["wq"]),
            (ab.t[:, self.o_wv:self.o_wv + 8192], T["wv"]),
            (ab.t[:, self.o_ident:self.o_ident + 128], T["ident"]),
            (ab.t[:, self.o_wbeta:self.o_wbeta + 128], T["wbeta"]),
        ]
        for dst, src in bf_loads:
            c.op("g", "dma_start", dst, src, inc=[("gw", 16)])
        self.ld_q = c.snap("qc")
        self.ld_g = c.snap("gw")
        for e in ("t", "v", "s", "g"):
            c.wait(e, "qc", self.ld_q)

    def layernorm(self):
        c, T, af, ab, pb = self.kb, self.T, self.af, self.ab, self.pb
        xr = [af.t[:, self.o_xr + i * L:self.o_xr + (i + 1) * L] for i in range(2)]
        sqs = [af.t[:, self.o_sq + i * 512:self.o_sq + (i + 1) * 512] for i in range(2)]
        ones128 = af.t[:, self.o_ones:self.o_ones + 1]
        c.wait("t", "qc", self.ld_q)
        c.wait("v", "qc", self.ld_q)
        rel_t = [0, 0]
        sq_rel = [0, 0]
        for chk in range(8):
            b = chk % 2
            c.wait("q", "t", rel_t[b])
            c.op("q", "dma_start", xr[b], T["xT"][chk * 128:(chk + 1) * 128, :],
                 inc=[(f"qx{b}", 16)])
            ldq = c.snap(f"qx{b}")
            for tt in range(4):
                sb = (chk * 4 + tt) % 2
                cols = slice(tt * 512, (tt + 1) * 512)
                c.wait("s", f"qx{b}", ldq)
                c.wait("s", "t", sq_rel[sb])
                c.op("s", "activation", sqs[sb], xr[b][:, cols], AF.Square, inc=[("s", 1)])
                ssn = c.snap("s")
                c.wait("t", f"qx{b}", ldq)
                c.wait("t", "s", ssn)
                c.op("t", "matmul", pb[tt][0:1, :], ones128, xr[b][:, cols],
                     start=(chk == 0), stop=(chk == 7))
                c.op("t", "matmul", pb[4 + tt][0:1, :], ones128, sqs[sb],
                     start=(chk == 0), stop=(chk == 7), inc=[("t", 1)])
                sq_rel[sb] = c.snap("t")
            rel_t[b] = c.snap("t")
        stats_done = c.snap("t")
        mrow = af.t[0:1, self.o_sm + 0:self.o_sm + 512]
        rrow = af.t[0:1, self.o_sm + 512:self.o_sm + 1024]
        c.wait("v", "t", stats_done)
        for tt in range(4):
            c.wait("v", "t", c.snap("t"))
            c.op("v", "tensor_scalar", mrow, pb[tt][0:1, :], 1.0 / DM, None, ALU.mult)
            c.op("v", "tensor_tensor", rrow, mrow, mrow, ALU.mult)
            c.op("v", "scalar_tensor_tensor", rrow, pb[4 + tt][0:1, :], 1.0 / DM, rrow,
                 ALU.mult, ALU.subtract, inc=[("v", 1)])
            vsn = c.snap("v")
            c.wait("s", "v", vsn)
            c.op("s", "activation", rrow, rrow, AF.Sqrt, bias=1e-5, scale=1.0, inc=[("s", 1)])
            c.wait("v", "s", c.snap("s"))
            c.op("v", "reciprocal", rrow, rrow, inc=[("v", 1)])
            vsn = c.snap("v")
            c.wait("t", "v", vsn)
            c.op("t", "matmul", pb[tt][:, :], self.ones1, mrow, start=True, stop=True)
            c.op("t", "matmul", pb[4 + tt][:, :], self.ones1, rrow, start=True, stop=True,
                 inc=[("t", 1)])
        bc_done = c.snap("t")
        xsc = af.t[:, self.o_xsc:self.o_xsc + 512]
        rel_v = [0, 0]
        for chk in range(8):
            b = chk % 2
            c.wait("q", "v", rel_v[b])
            c.wait("q", "t", stats_done)
            c.op("q", "dma_start", xr[b], T["xT"][chk * 128:(chk + 1) * 128, :],
                 inc=[(f"qx{b}", 16)])
            ldq = c.snap(f"qx{b}")
            c.wait("v", f"qx{b}", ldq)
            c.wait("v", "t", bc_done)
            for tt in range(4):
                cols = slice(tt * 512, (tt + 1) * 512)
                c.op("v", "tensor_tensor", xsc, xr[b][:, cols], pb[tt][:, :], ALU.subtract)
                c.op("v", "tensor_tensor",
                     ab.t[:, self.o_xn + chk * L:self.o_xn + chk * L + L][:, cols],
                     xsc, pb[4 + tt][:, :], ALU.mult, inc=[("v", 1)])
            rel_v[b] = c.snap("v")
        self.xn_done_v = c.snap("v")
        self.ln_pb_free_v = c.snap("v")

    def coef_build(self):
        c, T, af = self.kb, self.T, self.af
        xr = self.o_xr
        num = af.t[:, xr:xr + NCH]
        den = af.t[:, xr + NCH:xr + 2 * NCH]
        co = self.o_coef
        specs = [("a_num", "a_den", 0), ("wi_num", "wi_den", 2),
                 ("c1_num", "c1_den", 3), ("c2_num", "c2_den", 4)]
        for nn, dd, slot in specs:
            c.wait("q", "v", c.snap("v"))
            c.op("q", "dma_start", num, T[nn], inc=[("qk", 16)])
            c.op("q", "dma_start", den, T[dd], inc=[("qk", 16)])
            ldq = c.snap("qk")
            c.wait("v", "qk", ldq)
            den2 = af.t[:, xr + 2 * NCH:xr + 3 * NCH]
            c.op("v", "reciprocal", den2, den, inc=[("v", 1)])
            c.wait("v", "v", c.snap("v"))
            c.op("v", "tensor_tensor", af.t[:, co + slot * NCH:co + (slot + 1) * NCH],
                 num, den2, ALU.mult, inc=[("v", 1)])
            if slot == 0 and "cdbg" in self.dbg:
                c.wait("g", "v", c.snap("v"))
                c.op("g", "dma_start", self.dbg["cdbg"], af.t[:, xr:xr + 3 * NCH],
                     inc=[("og", 16)])
                c.wait("v", "og", c.snap("og"))
        c.op("v", "tensor_scalar", af.t[:, co + 1 * NCH:co + 2 * NCH],
             af.t[:, co + 0 * NCH:co + 1 * NCH], 1.0, -1.0,
             ALU.subtract, ALU.mult, inc=[("v", 1)])
        c.wait("q", "v", c.snap("v"))
        c.op("q", "dma_start", af.t[:, self.o_tmT:self.o_tmT + NCH * 64], T["tnumT"],
             inc=[("qk", 16)])
        trow = af.t[0:1, xr:xr + NCH * 64]
        c.op("q", "dma_start", trow, T["tden"], inc=[("qk", 16)])
        ldq = c.snap("qk")
        c.wait("v", "qk", ldq)
        c.op("v", "reciprocal", trow, trow, inc=[("v", 1)])
        vsn = c.snap("v")
        rrow2 = af.t[0:1, xr + NCH * 64:xr + NCH * 64 + 512]
        c.op("q", "dma_start", rrow2, T["rmsw"], inc=[("qk", 16)])
        ldq2 = c.snap("qk")
        pb = self.pb
        c.wait("t", "v", vsn)
        c.wait("t", "qk", ldq2)
        for i in range(4):
            c.op("t", "matmul", pb[7][:, :], self.ones1,
                 trow[:, i * 512:(i + 1) * 512], start=True, stop=True, inc=[("t", 1)])
            tsn = c.snap("t")
            c.wait("v", "t", tsn)
            c.op("v", "tensor_tensor",
                 af.t[:, self.o_tmT + i * 512:self.o_tmT + (i + 1) * 512],
                 af.t[:, self.o_tmT + i * 512:self.o_tmT + (i + 1) * 512],
                 pb[7][:, :], ALU.mult, inc=[("v", 1)])
            c.wait("t", "v", c.snap("v"))
        c.op("t", "matmul", pb[7][:, :], self.ones1, rrow2, start=True, stop=True,
             inc=[("t", 1)])
        c.wait("v", "t", c.snap("t"))
        c.op("v", "tensor_copy", af.t[:, self.o_rmswb:self.o_rmswb + 512], pb[7][:, :],
             inc=[("v", 1)])
        self.coef_done_v = c.snap("v")
        if "coef2" in self.dbg:
            c.wait("g", "v", self.coef_done_v)
            c.op("g", "dma_start", self.dbg["coef2"],
                 af.t[:, self.o_coef:self.o_coef + 6 * NCH], inc=[("og", 16)])

    def beta_proj(self):
        c, T, af, ab, pb = self.kb, self.T, self.af, self.ab, self.pb
        if self.stage == "beta":
            c.wait("g", "gw", self.ld_g)
            c.op("g", "dma_start", self.dbg["wb2"],
                 ab.t[:, self.o_wbeta:self.o_wbeta + 128], inc=[("og", 16)])
        c.wait("t", "v", self.xn_done_v)
        c.wait("t", "gw", self.ld_g)
        for tt in range(4):
            cols = slice(tt * 512, (tt + 1) * 512)
            for d in range(8):
                c.op("t", "matmul", pb[6][0:16, :],
                     ab.t[:, self.o_wbeta + d * 16:self.o_wbeta + (d + 1) * 16],
                     ab.t[:, self.o_xn + d * L + tt * 512:self.o_xn + d * L + tt * 512 + 512],
                     start=(d == 0), stop=(d == 7), inc=[("t", 1)] if d == 7 else ())
            tsn = c.snap("t")
            c.wait("s", "t", tsn)
            c.wait("s", "qc", self.ld_q)
            c.op("s", "activation", af.t[0:16, self.o_beta + tt * 512:self.o_beta + (tt + 1) * 512],
                 pb[6][0:16, :], AF.Sigmoid, bias=self.bbeta, scale=1.0, inc=[("s", 1)])
            if self.stage == "beta":
                c.wait("v", "t", tsn)
                c.wait("v", "s", c.snap("s"))
                c.op("v", "tensor_copy",
                     af.t[0:16, self.o_xr + tt * 512:self.o_xr + (tt + 1) * 512],
                     pb[6][0:16, :], inc=[("v", 1)])
                c.wait("t", "v", c.snap("v"))
            c.wait("t", "s", c.snap("s"))
        self.beta_done_s = c.snap("s")

    def preproc_block(self, b):
        c, T, af, ab, pb = self.kb, self.T, self.af, self.ab, self.pb
        cols = slice(b * BLKP, (b + 1) * BLKP)
        wofs = {0: self.o_wk, 1: self.o_wq, 2: self.o_wv}
        bofs = {0: self.o_bias + 0, 1: self.o_bias + 8, 2: self.o_bias + 16}
        cvof = {0: self.o_convw + 0, 1: self.o_convw + 32, 2: self.o_convw + 64}
        if b == 0:
            c.op("v", "memset", af.t[:, self.o_tail:self.o_tail + 72], 0.0,
                 inc=[("v", 1)])
            self._pp_conv_v = c.snap("v")
        for ch in range(8):
            for ti in range(3):
                reg = pb[5][:, ti * 128:(ti + 1) * 128]
                if ch == 0 and b == 0 and ti == 0:
                    c.wait("t", "v", self.xn_done_v)
                c.wait("t", "v", self._dw_bank_v[5])
                c.wait("t", "s", self._pp_evac_s)
                for d in range(8):
                    c.op("t", "matmul", reg,
                         ab.t[:, wofs[ti] + d * 1024 + ch * 128:wofs[ti] + d * 1024 + (ch + 1) * 128],
                         ab.t[:, self.o_xn + d * L + b * BLKP:self.o_xn + d * L + (b + 1) * BLKP],
                         start=(d == 0), stop=(d == 7), inc=[("t", 1)] if d == 7 else ())
            tsn = c.snap("t")
            self._last_proj_t = tsn
            scr = [af.t[:, self.o_scr[i]:self.o_scr[i] + 132] for i in range(3)]
            c.wait("s", "t", tsn)
            c.wait("s", "v", self._pp_conv_v)
            for ti in range(3):
                c.op("s", "activation", scr[ti][:, 3:131], pb[5][:, ti * 128:(ti + 1) * 128],
                     AF.Identity, bias=af.t[:, bofs[ti] + ch:bofs[ti] + ch + 1], scale=1.0,
                     inc=[("s", 1)] if ti == 2 else ())
            self._pp_evac_s = c.snap("s")
            co = [af.t[:, self.o_co[i]:self.o_co[i] + 128] for i in range(3)]
            dst = [af.t[:, self.o_coK + ch * 128:self.o_coK + (ch + 1) * 128],
                   af.t[:, self.o_coQ + ch * 128:self.o_coQ + (ch + 1) * 128],
                   af.t[:, self.o_vs + ch * 128:self.o_vs + (ch + 1) * 128]]
            for ti in range(3):
                cw = af.t[:, cvof[ti] + ch * 4:cvof[ti] + ch * 4 + 4]
                tgt = co[ti] if ti < 2 else co[2]
                tl = af.t[:, self.o_tail + ti * 24 + ch * 3:self.o_tail + ti * 24 + ch * 3 + 3]
                c.wait("v", "s", self._pp_evac_s)
                c.wait("v", "s", self._pp_sil_s)
                c.op("v", "tensor_copy", scr[ti][:, 0:3], tl)
                c.op("v", "tensor_scalar", tgt, scr[ti][:, 3:131], cw[:, 3:4], None, ALU.mult)
                for j in (2, 1, 0):
                    c.op("v", "scalar_tensor_tensor", tgt, scr[ti][:, j:j + 128],
                         cw[:, j:j + 1], tgt, ALU.mult, ALU.add)
                c.op("v", "tensor_copy", tl, scr[ti][:, 128:131],
                     inc=[("v", 1)] if ti == 2 else ())
            self._pp_conv_v = c.snap("v")
            kk = af.t[:, self.o_kk:self.o_kk + 128]
            for ti in range(2):
                c.wait("s", "v", self._pp_conv_v)
                c.wait("s", "t", self._pp_red_t)
                c.op("s", "activation", kk, co[ti], AF.Square, inc=[("s", 1)])
                ssn = c.snap("s")
                c.wait("t", "s", ssn)
                if ch == 0 and ti == 0:
                    c.wait("t", "v", self._pp_norm_v)
                    c.wait("t", "v", self._dw_bank_v[6])
                c.op("t", "matmul", pb[6][0:16, ti * 128:(ti + 1) * 128],
                     af.t[:, self.o_h16 + ch * 16:self.o_h16 + (ch + 1) * 16],
                     kk, start=(ch == 0 and ti == 0), stop=(ch == 7 and ti == 1),
                     inc=[("t", 1)])
                self._pp_red_t = c.snap("t")
            c.op("v", "tensor_copy", dst[0], co[0])
            c.op("v", "tensor_copy", dst[1], co[1], inc=[("v", 1)])
            self._pp_conv_v = c.snap("v")
            c.wait("s", "v", self._pp_conv_v)
            c.wait("s", "v", self._pp_mul_v)
            c.op("s", "activation", dst[2], co[2], AF.Silu, inc=[("s", 1)])
            self._pp_sil_s = c.snap("s")
        red_done = self._pp_red_t
        rn = af.t[0:16, self.o_rn:self.o_rn + 256]
        if self.stage == "pre" and b == 0:
            c.wait("v", "t", red_done)
            c.op("v", "tensor_copy", af.t[0:16, self.o_ost:self.o_ost + 256],
                 pb[6][0:16, 0:256], inc=[("v", 1)])
            c.wait("q", "v", c.snap("v"))
            c.op("q", "dma_start", self.dbg["ns"], af.t[0:16, self.o_ost:self.o_ost + 256],
                 inc=[("oq", 16)])
            c.wait("s", "v", c.snap("v"))
        c.wait("s", "t", red_done)
        c.op("s", "activation", rn, pb[6][0:16, 0:256], AF.Sqrt, inc=[("s", 1)])
        ssn = c.snap("s")
        c.wait("v", "s", ssn)
        c.op("v", "tensor_scalar", rn, rn, 1e-6, None, ALU.add)
        c.op("v", "reciprocal", rn, rn, inc=[("v", 1)])
        self._pp_norm_v = c.snap("v")
        norm_v = self._pp_norm_v
        slot = (b % 2) * BLKP
        c.wait("v", "t", self._ring_free_t[b % 2])
        for ch in range(8):
            e16c = af.t[0:16, self.o_e16 + ch * 128:self.o_e16 + (ch + 1) * 128]
            c.wait("t", "v", norm_v)
            c.wait("t", "v", self._pp_mul_v)
            c.wait("t", "v", self._dw_bank_v[7])
            c.wait("t", "s", self.beta_done_s)
            c.op("t", "matmul", pb[7][:, 0:128], e16c, rn[:, 0:128], start=True, stop=True)
            c.op("t", "matmul", pb[7][:, 128:256], e16c, rn[:, 128:256], start=True, stop=True)
            c.op("t", "matmul", pb[7][:, 256:384], e16c,
                 af.t[0:16, self.o_beta + b * BLKP:self.o_beta + (b + 1) * BLKP],
                 start=True, stop=True, inc=[("t", 1)])
            tsn = c.snap("t")
            c.wait("v", "t", tsn)
            c.wait("v", "s", self._pp_sil_s)
            ring = [(self.o_rk, self.o_coK), (self.o_rq, self.o_coQ), (self.o_rv, self.o_vs)]
            for ti, (ro, so) in enumerate(ring):
                c.op("v", "tensor_tensor",
                     ab.t[:, ro + ch * 2 * BLKP + slot:ro + ch * 2 * BLKP + slot + BLKP],
                     af.t[:, so + ch * 128:so + (ch + 1) * 128],
                     pb[7][:, ti * 128:(ti + 1) * 128], ALU.mult,
                     inc=[("v", 1)] if ti == 2 else ())
            self._pp_mul_v = c.snap("v")
        self._pp_ring_v = c.snap("v")

    def dbg_store_pre(self, b):
        c, ab, af = self.kb, self.ab, self.af
        slot = (b % 2) * BLKP
        cols = slice(b * BLKP, (b + 1) * BLKP)
        if b == 0:
            c.wait("q", "s", self.beta_done_s)
            c.op("q", "dma_start", self.dbg["beta"], af.t[0:16, self.o_beta:self.o_beta + L],
                 inc=[("oq", 16)])
        c.wait("q", "v", self._pp_ring_v)
        for ch in range(8):
            for nm, so in (("ck", self.o_coK), ("cq", self.o_coQ), ("cv", self.o_vs)):
                c.op("q", "dma_start", self.dbg[nm][ch * 128:(ch + 1) * 128, cols],
                     af.t[:, so + ch * 128:so + (ch + 1) * 128], inc=[("oq", 16)])
        c.op("q", "dma_start", self.dbg["rn"][:, b * 256:(b + 1) * 256],
             af.t[0:16, self.o_rn:self.o_rn + 256], inc=[("oq", 16)])
        c.wait("g", "v", self._pp_ring_v)
        for ch in range(8):
            for nm, ro in (("kh", self.o_rk), ("qh", self.o_rq), ("vh", self.o_rv)):
                c.op("g", "dma_start", self.dbg[nm][ch * 128:(ch + 1) * 128, cols],
                     ab.t[:, ro + ch * 2 * BLKP + slot:ro + ch * 2 * BLKP + slot + BLKP],
                     inc=[("og", 16)])
        self._dbg_pre_o = c.snap("oq")

    def bc8(self, arena, col, f=64, reps=8):
        n = arena.cols
        return bass.AP(arena.t, col, [[n, 128], [0, reps], [1, f]])

    def delta_init(self):
        c, af, ab = self.kb, self.af, self.ab
        c.op("v", "memset", af.t[:, self.o_st32:self.o_st32 + 1024], 0.0)
        c.op("v", "memset", ab.t[:, self.o_stb:self.o_stb + 1024], 0.0, inc=[("v", 1)])
        self._st_v = c.snap("v")
        self._orng_rd = 0

    def _qgroup(self, out_region, lhsT_of, rhs_of, start, stop, n, inc_last=True,
                nbanks=1, per_pair=False):
        c = self.kb
        per = 8 // nbanks
        for p in range(8):
            for h in range(2):
                rows = slice(h * 64, (h + 1) * 64)
                st = True if per_pair else (start and p % per == 0)
                sp = True if per_pair else (stop and p % per == per - 1)
                c.op("t", "matmul", out_region(p)[rows, :],
                     lhsT_of(p, h), rhs_of(p, h),
                     start=st, stop=sp,
                     tile_position=(h * 64, h * 64),
                     skip_group_check=True,
                     inc=[("t", 1)] if (inc_last and p == 7 and h == 1) else ())

    def delta_chunk(self, i):
        c, T, af, ab, pb = self.kb, self.T, self.af, self.ab, self.pb
        b = i // 2
        rco = (b % 2) * BLKP + (i % 2) * 64
        kt = lambda p: ab.t[:, self.o_rk + p * 256 + rco:self.o_rk + p * 256 + rco + 64]
        qt = lambda p: ab.t[:, self.o_rq + p * 256 + rco:self.o_rq + p * 256 + rco + 64]
        vt = lambda p: ab.t[:, self.o_rv + p * 256 + rco:self.o_rv + p * 256 + rco + 64]
        bns = lambda p: ab.t[:, self.o_bns + p * 64:self.o_bns + (p + 1) * 64]
        idn = ab.t[:, self.o_ident:self.o_ident + 128]
        X = [self.o_X + j * 512 for j in range(3)] + [self.o_bns]
        Xt = lambda sl, p: ab.t[:, X[sl] + p * 64:X[sl] + (p + 1) * 64]
        ring_ready = self._pp_ring_v

        c.wait("t", "v", self._dw_bank_v[7])
        c.wait("t", "v", self._pp_mul_v)
        c.wait("t", "s", self.beta_done_s)
        for p in range(8):
            c.op("t", "matmul", pb[7][:, p * 64:(p + 1) * 64],
                 af.t[0:16, self.o_e16n + p * 128:self.o_e16n + (p + 1) * 128],
                 af.t[0:16, self.o_beta + i * 64:self.o_beta + (i + 1) * 64],
                 start=True, stop=True, skip_group_check=True,
                 inc=[("t", 1)] if p == 7 else ())
        t_bc = c.snap("t")
        c.wait("v", "t", t_bc)
        c.wait("v", "t", self._bns_free_t)
        c.wait("v", "v", ring_ready)
        kv = ab.t[:, self.o_rk:self.o_rk + 2048].rearrange("p (n f) -> p n f", f=256)[:, :, rco:rco + 64]
        c.op("v", "tensor_tensor",
             ab.t[:, self.o_bns:self.o_bns + 512].rearrange("p (n f) -> p n f", f=64),
             kv, pb[7][:, :].rearrange("p (n f) -> p n f", f=64), ALU.mult, inc=[("v", 1)])
        self._dw_bank_v[7] = c.snap("v")
        bns_v = c.snap("v")

        c.wait("t", "v", bns_v)
        for bank, (lf, rf) in ((4, (kt, bns)), (5, (bns, kt)), (6, (kt, qt))):
            c.wait("t", "v", self._dw_bank_v[bank])
            self._qgroup(lambda p, _b=bank: pb[_b][:, p * 64:(p + 1) * 64],
                         lambda p, h, _lf=lf: _lf(p)[h * 64:(h + 1) * 64, :],
                         lambda p, h, _rf=rf: _rf(p)[h * 64:(h + 1) * 64, :],
                         start=True, stop=True, n=64, per_pair=True)
        t_zsm = c.snap("t")
        msku = self.bc8(self.af, self.o_msk + 64)
        mskl = self.bc8(self.af, self.o_msk)
        tmt = self.bc8(self.af, self.o_tmT + i * 64)
        c.wait("v", "t", t_zsm)
        r3 = lambda off: ab.t[:, off:off + 512].rearrange("p (n f) -> p n f", f=64)
        c.op("v", "tensor_tensor", r3(X[1]), pb[4][:, :].rearrange("p (n f) -> p n f", f=64),
             msku, ALU.mult)
        c.op("v", "tensor_tensor", r3(X[0]), pb[5][:, :].rearrange("p (n f) -> p n f", f=64),
             mskl, ALU.mult)
        c.op("v", "tensor_tensor",
             ab.t[:, self.o_mt:self.o_mt + 512].rearrange("p (n f) -> p n f", f=64),
             pb[6][:, :].rearrange("p (n f) -> p n f", f=64), tmt, ALU.mult, inc=[("v", 1)])
        self._dw_bank_v[4] = self._dw_bank_v[5] = self._dw_bank_v[6] = c.snap("v")
        x_v = c.snap("v")

        c.wait("t", "v", x_v)
        self._qgroup(lambda p: pb[4][:, p * 64:(p + 1) * 64],
                     lambda p, h: kt(p)[h * 64:(h + 1) * 64, :],
                     lambda p, h: idn[h * 64:(h + 1) * 64, h * 64:h * 64 + 64],
                     start=True, stop=True, n=64, per_pair=True)
        t_ktm = c.snap("t")
        c.wait("v", "t", t_ktm)
        c.op("v", "tensor_copy",
             ab.t[:, self.o_ktm:self.o_ktm + 512], pb[4][:, 0:512], inc=[("v", 1)])
        self._dw_bank_v[4] = c.snap("v")

        chain = lambda p: pb[p // 2][:, (p % 2) * 128:(p % 2) * 128 + 128]
        for bk in range(4):
            c.wait("t", "v", self._dw_bank_v[bk])
        self._qgroup(lambda p: chain(p)[:, 0:64],
                     lambda p, h: vt(p)[h * 64:(h + 1) * 64, :],
                     lambda p, h: idn[h * 64:(h + 1) * 64, h * 64:h * 64 + 64],
                     start=True, stop=False, n=64, nbanks=4)
        self._qgroup(lambda p: chain(p)[:, 64:128],
                     lambda p, h: bns(p)[h * 64:(h + 1) * 64, :],
                     lambda p, h: idn[h * 64:(h + 1) * 64, h * 64:h * 64 + 64],
                     start=False, stop=False, n=64)
        t_rhs0 = c.snap("t")

        sl_t, sl, sl_free = 1, 0, 2
        rhs_sb = ab.t[:, self.o_rhs:self.o_rhs + 1024]
        t_prev_chain = t_rhs0
        for k in range(6):
            c.wait("v", "t", t_prev_chain)
            for bk in range(4):
                c.op("v", "tensor_copy", rhs_sb[:, bk * 256:(bk + 1) * 256],
                     pb[bk][:, 0:256], inc=[("v", 1)] if bk == 3 else ())
            rhs_v = c.snap("v")
            c.wait("t", "v", rhs_v)
            self._qgroup(lambda p: chain(p),
                         lambda p, h, _s=sl_t: Xt(_s, p)[h * 64:(h + 1) * 64, :],
                         lambda p, h: rhs_sb[h * 64:(h + 1) * 64, p * 128:(p + 1) * 128],
                         start=False, stop=False, n=128)
            t_prev_chain = c.snap("t")
            if k == 5:
                break
            c.wait("t", "v", self._dw_bank_v[5])
            if k < 4:
                self._qgroup(lambda p: pb[5][:, p * 64:(p + 1) * 64],
                             lambda p, h, _s=sl_t: Xt(_s, p)[h * 64:(h + 1) * 64, :],
                             lambda p, h, _s=sl: Xt(_s, p)[h * 64:(h + 1) * 64, :],
                             start=True, stop=True, n=64, per_pair=True)
                t_sq = c.snap("t")
                c.wait("v", "t", t_sq)
                c.op("v", "tensor_copy", ab.t[:, X[sl_free]:X[sl_free] + 512],
                     pb[5][:, :], inc=[("v", 1)])
                self._dw_bank_v[5] = c.snap("v")
                xk1 = sl_free
                xk1_v = c.snap("v")
                c.wait("t", "v", xk1_v)
                c.wait("t", "v", self._dw_bank_v[6])
                self._qgroup(lambda p: pb[6][:, p * 64:(p + 1) * 64],
                             lambda p, h, _s=xk1: Xt(_s, p)[h * 64:(h + 1) * 64, :],
                             lambda p, h: idn[h * 64:(h + 1) * 64, h * 64:h * 64 + 64],
                             start=True, stop=True, n=64, per_pair=True)
                t_tr = c.snap("t")
                newXt = sl
                c.wait("v", "t", t_tr)
                c.wait("v", "t", t_prev_chain)
                c.op("v", "tensor_copy", ab.t[:, X[newXt]:X[newXt] + 512],
                     pb[6][:, :], inc=[("v", 1)])
                self._dw_bank_v[6] = c.snap("v")
                sl_free = sl_t
                sl_t = newXt
                sl = xk1
            else:
                self._qgroup(lambda p: pb[5][:, p * 64:(p + 1) * 64],
                             lambda p, h, _s=sl: Xt(_s, p)[h * 64:(h + 1) * 64, :],
                             lambda p, h, _s=sl_t: Xt(_s, p)[h * 64:(h + 1) * 64, :],
                             start=True, stop=True, n=64, per_pair=True)
                t_sq = c.snap("t")
                c.wait("v", "t", t_sq)
                c.wait("v", "t", t_prev_chain)
                c.op("v", "tensor_copy", ab.t[:, X[sl_free]:X[sl_free] + 512],
                     pb[5][:, :], inc=[("v", 1)])
                self._dw_bank_v[5] = c.snap("v")
                sl_t = sl_free
        c.wait("v", "t", t_prev_chain)
        pg = ab.t[:, self.o_pg:self.o_pg + 1024]
        for bk in range(4):
            c.op("v", "tensor_copy", pg[:, bk * 256:(bk + 1) * 256],
                 pb[bk][:, 0:256], inc=[("v", 1)] if bk == 3 else ())
        pg_v = c.snap("v")
        pgP = lambda p: pg[:, p * 128:p * 128 + 64]
        pgG = lambda p: pg[:, p * 128 + 64:p * 128 + 128]

        mts = lambda p: ab.t[:, self.o_mt + p * 64:self.o_mt + (p + 1) * 64]
        c.wait("t", "v", pg_v)
        c.wait("t", "v", self._dw_bank_v[4])
        self._qgroup(lambda p: pb[4][:, p * 64:(p + 1) * 64],
                     lambda p, h: pgG(p)[h * 64:(h + 1) * 64, :],
                     lambda p, h: idn[h * 64:(h + 1) * 64, h * 64:h * 64 + 64],
                     start=True, stop=True, n=64, per_pair=True)
        c.wait("t", "v", self._dw_bank_v[5])
        self._qgroup(lambda p: pb[5][:, p * 64:(p + 1) * 64],
                     lambda p, h: pgP(p)[h * 64:(h + 1) * 64, :],
                     lambda p, h: mts(p)[h * 64:(h + 1) * 64, :],
                     start=True, stop=True, n=64, per_pair=True)
        c.wait("t", "v", self._dw_bank_v[6])
        self._qgroup(lambda p: pb[6][:, p * 64:(p + 1) * 64],
                     lambda p, h: pgG(p)[h * 64:(h + 1) * 64, :],
                     lambda p, h: mts(p)[h * 64:(h + 1) * 64, :],
                     start=True, stop=True, n=64, per_pair=True)
        t_gm = c.snap("t")
        c.wait("v", "t", t_gm)
        c.op("v", "tensor_copy", ab.t[:, self.o_gfm:self.o_gfm + 512], pb[4][:, :])
        c.op("v", "tensor_copy", ab.t[:, self.o_mpf:self.o_mpf + 512], pb[5][:, :])
        c.op("v", "tensor_copy", ab.t[:, self.o_mgt:self.o_mgt + 512], pb[6][:, :],
             inc=[("v", 1)])
        self._dw_bank_v[4] = self._dw_bank_v[5] = self._dw_bank_v[6] = c.snap("v")
        gm_v = c.snap("v")

        stb = ab.t[:, self.o_stb:self.o_stb + 1024]
        stbW = lambda p: stb[:, p * 128:p * 128 + 64]
        c.wait("t", "v", self._st_v)
        c.wait("t", "v", self._dw_bank_v[4])
        c.wait("t", "v", self._dw_bank_v[5])
        qw = lambda p: pb[4 + p // 4][:, (p % 4) * 128:(p % 4) * 128 + 128]
        self._qgroup(lambda p: qw(p),
                     lambda p, h: qt(p)[h * 64:(h + 1) * 64, :],
                     lambda p, h: stb[h * 64:(h + 1) * 64, p * 128:(p + 1) * 128],
                     start=True, stop=True, n=128, per_pair=True)
        c.wait("t", "v", gm_v)
        c.wait("t", "v", self._dw_bank_v[6])
        mpf = lambda p: ab.t[:, self.o_mpf + p * 64:self.o_mpf + (p + 1) * 64]
        mgt = lambda p: ab.t[:, self.o_mgt + p * 64:self.o_mgt + (p + 1) * 64]
        gfm = lambda p: ab.t[:, self.o_gfm + p * 64:self.o_gfm + (p + 1) * 64]
        for p in range(8):
            for h in range(2):
                rows = slice(h * 64, (h + 1) * 64)
                c.op("t", "matmul", pb[6][:, p * 64:(p + 1) * 64][rows, :],
                     mpf(p)[h * 64:(h + 1) * 64, :],
                     idn[h * 64:(h + 1) * 64, h * 64:h * 64 + 64],
                     start=True, stop=False, tile_position=(h * 64, h * 64),
                     skip_group_check=True)
            for h in range(2):
                rows = slice(h * 64, (h + 1) * 64)
                c.op("t", "matmul", pb[6][:, p * 64:(p + 1) * 64][rows, :],
                     mgt(p)[h * 64:(h + 1) * 64, :],
                     stbW(p)[h * 64:(h + 1) * 64, :],
                     start=False, stop=True, tile_position=(h * 64, h * 64),
                     skip_group_check=True,
                     inc=[("t", 1)] if (p == 7 and h == 1) else ())
        self._qgroup(lambda p: chain(p)[:, 0:64],
                     lambda p, h: gfm(p)[h * 64:(h + 1) * 64, :],
                     lambda p, h: stbW(p)[h * 64:(h + 1) * 64, :],
                     start=False, stop=True, n=64, nbanks=4)
        t_u = c.snap("t")
        c.wait("v", "t", t_u)
        usb = ab.t[:, self.o_usb:self.o_usb + 512]
        uwb = ab.t[:, self.o_uwb:self.o_uwb + 512]
        for bk in range(4):
            c.op("v", "tensor_copy",
                 usb[:, bk * 128:(bk + 1) * 128].rearrange("p (n f) -> p n f", f=64),
                 pb[bk][:, 0:256].rearrange("p (n f) -> p n f", f=128)[:, :, 0:64],
                 inc=[("v", 1)] if bk == 3 else ())
        c.op("v", "tensor_scalar", uwb, usb,
             af.t[:, self.o_coef + 2 * NCH + i:self.o_coef + 2 * NCH + i + 1], None,
             ALU.mult, inc=[("v", 1)])
        u_v = c.snap("v")
        for bk in range(4):
            self._dw_bank_v[bk] = u_v

        ktm = lambda p: ab.t[:, self.o_ktm + p * 64:self.o_ktm + (p + 1) * 64]
        usbp = lambda p: usb[:, p * 64:(p + 1) * 64]
        uwbp = lambda p: uwb[:, p * 64:(p + 1) * 64]
        c.wait("t", "v", u_v)
        self._qgroup(lambda p: pb[0][:, p * 64:(p + 1) * 64],
                     lambda p, h: ktm(p)[h * 64:(h + 1) * 64, :],
                     lambda p, h: usbp(p)[h * 64:(h + 1) * 64, :],
                     start=True, stop=True, n=64, per_pair=True)
        self._qgroup(lambda p: pb[1][:, p * 64:(p + 1) * 64],
                     lambda p, h: ktm(p)[h * 64:(h + 1) * 64, :],
                     lambda p, h: uwbp(p)[h * 64:(h + 1) * 64, :],
                     start=True, stop=True, n=64, per_pair=True)
        t_wd = c.snap("t")

        if self.stage == "delta_dbg" and i == 0:
            c.wait("v", "t", t_wd)
            c.op("v", "tensor_copy", af.t[:, self.o_xr:self.o_xr + 512], pb[4][:, :])
            c.op("v", "tensor_copy", af.t[:, self.o_xr + 512:self.o_xr + 1024], pb[5][:, :])
            c.op("v", "tensor_copy", af.t[:, self.o_xr + 1024:self.o_xr + 1536], pb[6][:, :],
                 inc=[("v", 1)])
            c.wait("g", "v", c.snap("v"))
            c.op("g", "dma_start", self.dbg["z_qwmu"], af.t[:, self.o_xr:self.o_xr + 1536],
                 inc=[("og", 16)])
            c.wait("v", "og", c.snap("og"))
        ost = af.t[:, self.o_ost:self.o_ost + 1024]
        acol = af.t[:, self.o_coef + 0 * NCH + i:self.o_coef + 0 * NCH + i + 1]
        ccol = af.t[:, self.o_coef + 1 * NCH + i:self.o_coef + 1 * NCH + i + 1]
        c.wait("v", "t", t_wd)
        c.op("v", "tensor_copy", ost[:, 512:1024], pb[6][:, :])
        for half in range(2):
            qs = pb[4 + half][:, :].rearrange("p (n f) -> p n f", f=128)
            c.op("v", "scalar_tensor_tensor",
                 ost[:, half * 256:(half + 1) * 256].rearrange("p (n f) -> p n f", f=64),
                 qs[:, :, 64:128], acol, ost[:, 512 + half * 256:512 + (half + 1) * 256]
                 .rearrange("p (n f) -> p n f", f=64), ALU.mult, ALU.add)
        for half in range(2):
            qs = pb[4 + half][:, :].rearrange("p (n f) -> p n f", f=128)
            c.op("v", "scalar_tensor_tensor",
                 ost[:, 512 + half * 256:512 + (half + 1) * 256].rearrange("p (n f) -> p n f", f=64),
                 qs[:, :, 0:64], ccol, ost[:, half * 256:(half + 1) * 256]
                 .rearrange("p (n f) -> p n f", f=64), ALU.mult, ALU.add,
                 inc=[("v", 1)] if half == 1 else ())
        o_v = c.snap("v")
        self._dw_bank_v[4] = self._dw_bank_v[5] = self._dw_bank_v[6] = o_v
        st32 = af.t[:, self.o_st32:self.o_st32 + 1024]
        c2col = af.t[:, self.o_coef + 4 * NCH + i:self.o_coef + 4 * NCH + i + 1]
        c1col = af.t[:, self.o_coef + 3 * NCH + i:self.o_coef + 3 * NCH + i + 1]
        c.op("v", "scalar_tensor_tensor", ost[:, 0:512], st32[:, 0:512], c2col,
             pb[1][:, :], ALU.mult, ALU.add)
        c.op("v", "scalar_tensor_tensor", st32[:, 512:1024], st32[:, 512:1024], c1col,
             ost[:, 0:512], ALU.mult, ALU.add)
        c.op("v", "tensor_tensor", st32[:, 0:512], st32[:, 0:512], pb[0][:, :], ALU.add,
             inc=[("v", 1)])
        self._dw_bank_v[0] = self._dw_bank_v[1] = c.snap("v")
        c.op("v", "tensor_copy", stb.rearrange("p (n f) -> p n f", f=128)[:, :, 0:64],
             st32[:, 0:512].rearrange("p (n f) -> p n f", f=64))
        c.op("v", "tensor_copy", stb.rearrange("p (n f) -> p n f", f=128)[:, :, 64:128],
             st32[:, 512:1024].rearrange("p (n f) -> p n f", f=64), inc=[("v", 1)])
        self._st_v = c.snap("v")

        c.wait("s", "v", self._st_v)
        c.op("s", "activation", ost[:, 0:512], ost[:, 512:1024], AF.Square, inc=[("s", 1)])
        sq_s = c.snap("s")
        ssq = af.t[:, self.o_xsc:self.o_xsc + 8]
        c.wait("v", "s", sq_s)
        c.op("v", "tensor_reduce", ssq, ost[:, 0:512].rearrange("p (n f) -> p n f", f=64),
             AX.X, ALU.add, inc=[("v", 1)])
        c.wait("v", "v", c.snap("v"))
        c.op("v", "tensor_scalar", ssq, ssq, 1.0 / 64, None, ALU.mult, inc=[("v", 1)])
        c.wait("s", "v", c.snap("v"))
        c.op("s", "activation", ssq, ssq, AF.Sqrt, bias=1e-6, scale=1.0, inc=[("s", 1)])
        c.wait("v", "s", c.snap("s"))
        c.op("v", "reciprocal", ssq, ssq, inc=[("v", 1)])
        c.wait("v", "v", c.snap("v"))
        for p in range(8):
            c.op("v", "tensor_scalar", ost[:, p * 64:(p + 1) * 64],
                 ost[:, 512 + p * 64:512 + (p + 1) * 64], ssq[:, p:p + 1], None, ALU.mult)
        onb = ab.t[:, self.o_bns:self.o_bns + 512]
        c.op("v", "tensor_tensor", onb, ost[:, 0:512],
             af.t[:, self.o_rmswb:self.o_rmswb + 512], ALU.mult, inc=[("v", 1)])
        on_v = c.snap("v")
        c.wait("t", "v", on_v)
        c.wait("t", "v", self._dw_bank_v[2])
        onp = lambda p: onb[:, p * 64:(p + 1) * 64]
        self._qgroup(lambda p: pb[2][:, p * 64:(p + 1) * 64],
                     lambda p, h: onp(p)[h * 64:(h + 1) * 64, :],
                     lambda p, h: idn[h * 64:(h + 1) * 64, h * 64:h * 64 + 64],
                     start=True, stop=True, n=64, per_pair=True)
        t_ot = c.snap("t")
        self._bns_free_t = t_ot
        c.wait("v", "t", t_ot)
        oao = ab.t[:, self.o_ao:self.o_ao + 8 * L].rearrange("p (n f) -> p n f", f=L)
        c.op("v", "tensor_copy", oao[:, :, i * 64:(i + 1) * 64],
             pb[2][:, :].rearrange("p (n f) -> p n f", f=64),
             inc=[("v", 1)])
        self._dw_bank_v[2] = c.snap("v")
        self._o_done_v = c.snap("v")
        self._ring_free_t[(i // 2) % 2] = c.snap("t")

    def attn_proj(self):
        c, T, af, ab, pb = self.kb, self.T, self.af, self.ab, self.pb
        self.o_wout = self.o_wk
        c.wait("g", "t", self._last_proj_t)
        c.op("g", "dma_start", ab.t[:, self.o_wout:self.o_wout + 8192], T["wout"],
             inc=[("gwo", 16)])
        ldg = c.snap("gwo")
        c.wait("t", "gwo", ldg)
        c.wait("t", "v", self._o_done_v)
        for tt in range(4):
            cols = slice(tt * 512, (tt + 1) * 512)
            for bank in range(8):
                c.wait("t", "v", self._dw_bank_v[bank])
            for o in range(8):
                for p in range(8):
                    c.op("t", "matmul", pb[o][:, :],
                         ab.t[:, self.o_wout + p * 1024 + o * 128:self.o_wout + p * 1024 + (o + 1) * 128],
                         ab.t[:, self.o_ao + p * L + tt * 512:self.o_ao + p * L + tt * 512 + 512],
                         start=(p == 0), stop=(p == 7),
                         inc=[("t", 1)] if p == 7 else ())
            t_mm = c.snap("t")
            c.wait("s", "t", t_mm)
            for o in range(8):
                c.op("s", "activation",
                     ab.t[:, self.o_ao + o * L + tt * 512:self.o_ao + o * L + tt * 512 + 512],
                     pb[o][:, :], AF.Identity,
                     bias=af.t[:, self.o_bias + 24 + o:self.o_bias + 24 + o + 1], scale=1.0,
                     inc=[("s", 1)] if o == 7 else ())
            ssn = c.snap("s")
            for bank in range(8):
                self._dw_bank_v[bank] = 0
            c.wait("v", "s", ssn)
            c.op("v", "engine_nop", inc=[("v", 1)])
            for bank in range(8):
                self._dw_bank_v[bank] = c.snap("v")
        self._ao_done_s = c.snap("s")

    def router(self):
        c, T, af, ab, pb = self.kb, self.T, self.af, self.ab, self.pb
        c.wait("q", "s", self._ao_done_s)
        c.op("q", "dma_start", af.t[0:8, self.o_e8:self.o_e8 + 1024], T["e8"],
             inc=[("qc2", 16)])
        self.ld_e8 = c.snap("qc2")
        self.o_wr = self.o_w1sl
        wrf = af.t[:, self.o_lgt + 592:self.o_lgt + 656]
        c.op("q", "dma_start", wrf, T["wr"], inc=[("qc2", 16)])
        ldg = c.snap("qc2")
        c.wait("v", "qc2", ldg)
        c.wait("v", "s", self._ao_done_s)
        c.op("v", "tensor_copy", ab.t[:, self.o_wr:self.o_wr + 64], wrf, inc=[("v", 1)])
        wr_v = c.snap("v")
        c.wait("t", "v", wr_v)
        c.wait("t", "s", self._ao_done_s)
        c.wait("t", "v", self._dw_bank_v[7])
        for k in range(16):
            for d in range(8):
                c.op("t", "matmul", pb[7][0:128, k * 8:(k + 1) * 8],
                     ab.t[:, self.o_xn + d * L + k * 128:self.o_xn + d * L + (k + 1) * 128],
                     ab.t[:, self.o_wr + d * 8:self.o_wr + (d + 1) * 8],
                     start=(d == 0), stop=(d == 7), skip_group_check=True,
                     inc=[("t", 1)] if d == 7 else ())
        t_lg = c.snap("t")
        X0 = self.o_lgt
        lgt = af.t[:, X0:X0 + 128]
        c.wait("v", "t", t_lg)
        c.op("v", "tensor_copy", lgt, pb[7][:, 0:128], inc=[("v", 1)])
        lg3 = lgt.rearrange("p (k e) -> p k e", e=8)
        m1 = af.t[:, X0 + 128:X0 + 144]
        m2 = af.t[:, X0 + 144:X0 + 160]
        msk1 = af.t[:, X0 + 160:X0 + 288]
        msk2 = af.t[:, X0 + 288:X0 + 416]
        tmpl = af.t[:, X0 + 416:X0 + 544]
        c.op("v", "tensor_reduce", m1, lg3, AX.X, ALU.max, inc=[("v", 1)])
        c.wait("v", "v", c.snap("v"))
        for k in range(16):
            c.op("v", "tensor_scalar", msk1[:, k * 8:(k + 1) * 8],
                 lgt[:, k * 8:(k + 1) * 8], m1[:, k:k + 1], None, ALU.is_equal,
                 inc=[("v", 1)] if k == 15 else ())
        c.wait("v", "v", c.snap("v"))
        for k in range(16):
            c.op("v", "scalar_tensor_tensor", tmpl[:, k * 8:(k + 1) * 8],
                 msk1[:, k * 8:(k + 1) * 8], -60000.0, lgt[:, k * 8:(k + 1) * 8],
                 ALU.mult, ALU.add, inc=[("v", 1)] if k == 15 else ())
        c.wait("v", "v", c.snap("v"))
        c.op("v", "tensor_reduce", m2,
             tmpl.rearrange("p (k e) -> p k e", e=8), AX.X, ALU.max, inc=[("v", 1)])
        c.wait("v", "v", c.snap("v"))
        for k in range(16):
            c.op("v", "tensor_scalar", msk2[:, k * 8:(k + 1) * 8],
                 tmpl[:, k * 8:(k + 1) * 8], m2[:, k:k + 1], None, ALU.is_equal,
                 inc=[("v", 1)] if k == 15 else ())
        dd = af.t[:, X0 + 544:X0 + 560]
        s1 = af.t[:, X0 + 560:X0 + 576]
        s2 = af.t[:, X0 + 576:X0 + 592]
        c.wait("v", "v", c.snap("v"))
        c.op("v", "tensor_tensor", dd, m1, m2, ALU.subtract, inc=[("v", 1)])
        vsn = c.snap("v")
        c.wait("s", "v", vsn)
        c.op("s", "activation", s1, dd, AF.Sigmoid)
        c.op("s", "activation", s2, dd, AF.Sigmoid, scale=-1.0, inc=[("s", 1)])
        ssn = c.snap("s")
        c.wait("v", "s", ssn)
        rwtm = ab.t[:, self.o_w1sl + 64:self.o_w1sl + 192]
        for k in range(16):
            c.op("v", "tensor_scalar", tmpl[:, k * 8:(k + 1) * 8],
                 msk1[:, k * 8:(k + 1) * 8], s1[:, k:k + 1], None, ALU.mult,
                 inc=[("v", 1)] if k == 15 else ())
        c.wait("v", "v", c.snap("v"))
        for k in range(16):
            c.op("v", "scalar_tensor_tensor", rwtm[:, k * 8:(k + 1) * 8],
                 msk2[:, k * 8:(k + 1) * 8], s2[:, k:k + 1], tmpl[:, k * 8:(k + 1) * 8],
                 ALU.mult, ALU.add, inc=[("v", 1)] if k == 15 else ())
        c.wait("v", "v", c.snap("v"))
        rw_v = c.snap("v")
        idn = ab.t[:, self.o_ident:self.o_ident + 128]
        c.wait("t", "v", rw_v)
        for bk in range(4, 8):
            c.wait("t", "v", self._dw_bank_v[bk])
        for k in range(16):
            c.op("t", "matmul", pb[4 + k // 4][0:8, (k % 4) * 128:(k % 4) * 128 + 128],
                 rwtm[:, k * 8:(k + 1) * 8], idn, start=True, stop=True,
                 skip_group_check=True, inc=[("t", 1)] if k == 15 else ())
        t_rwt = c.snap("t")
        c.wait("v", "t", t_rwt)
        for bk in range(4):
            c.op("v", "tensor_copy",
                 af.t[0:8, self.o_rwfm + bk * 512:self.o_rwfm + (bk + 1) * 512],
                 pb[4 + bk][0:8, :], inc=[("v", 1)] if bk == 3 else ())
        self.rw_done_v = c.snap("v")
        for bk in range(4, 8):
            self._dw_bank_v[bk] = self.rw_done_v

    def ffn_half(self, half):
        c, T, af, ab, pb = self.kb, self.T, self.af, self.ab, self.pb
        hcols = lambda t2: slice(half * 1024 + t2 * 512, half * 1024 + (t2 + 1) * 512)
        c.wait("v", "oq", self._y_oq)
        c.op("v", "memset", af.t[:, self.o_acc:self.o_acc + 8192], 0.0, inc=[("v", 1)])
        acc0_v = c.snap("v")
        for e in range(E):
            c.wait("t", "v", self.rw_done_v)
            c.wait("t", "v", self._dw_bank_v[6])
            c.wait("t", "qc2", self.ld_e8)
            c.wait("t", "v", self._dw_bank_v[7])
            for t2 in range(2):
                c.op("t", "matmul", pb[6 + t2][:, :],
                     af.t[0:8, self.o_e8 + e * 128:self.o_e8 + (e + 1) * 128],
                     af.t[0:8, self.o_rwfm + half * 1024 + t2 * 512:
                          self.o_rwfm + half * 1024 + (t2 + 1) * 512],
                     start=True, stop=True, skip_group_check=True,
                     inc=[("t", 1)] if t2 == 1 else ())
            t_rwf = c.snap("t")
            c.wait("v", "t", t_rwf)
            c.op("v", "tensor_copy", af.t[:, self.o_rwf:self.o_rwf + 512], pb[6][:, :])
            c.op("v", "tensor_copy", af.t[:, self.o_rwf + 512:self.o_rwf + 1024],
                 pb[7][:, :], inc=[("v", 1)])
            rwf_v = c.snap("v")
            self._dw_bank_v[6] = self._dw_bank_v[7] = rwf_v
            c.wait("g", "t", self._w2_rd_t)
            c.op("g", "dma_start",
                 ab.t[:, self.o_w2b:self.o_w2b + 16384].rearrange("p (s f) -> p s f", f=1024),
                 T["w2s"][e * 2048:(e + 1) * 2048, :].rearrange("(s p) f -> p s f", p=128),
                 inc=[("g2", 16)])
            ld_w2 = c.snap("g2")
            for jp in range(16):
                slot = jp % 2
                sem = "g1a" if slot == 0 else "g1b"
                base = self.o_w1sl + 256 + slot * 2048
                c.wait("g", "t", self._w1_rd_t[slot])
                r0 = (e * 32 + jp) * 128
                r1 = (e * 32 + jp + 16) * 128
                c.op("g", "dma_start", ab.t[:, base:base + 1024],
                     T["w1s"][r0:r0 + 128, :], inc=[(sem, 16)])
                c.op("g", "dma_start", ab.t[:, base + 1024:base + 2048],
                     T["w1s"][r1:r1 + 128, :], inc=[(sem, 16)])
                ld1 = c.snap(sem)
                pp = slot * 2
                for t2 in range(2):
                    c.wait("t", sem, ld1)
                    c.wait("t", "v", self._h_bank_v[slot])
                    for d in range(8):
                        c.op("t", "matmul", pb[pp][:, :],
                             ab.t[:, base + d * 128:base + (d + 1) * 128],
                             ab.t[:, self.o_xn + d * L + hcols(t2).start:
                                  self.o_xn + d * L + hcols(t2).stop],
                             start=(d == 0), stop=(d == 7), skip_group_check=True)
                        c.op("t", "matmul", pb[pp + 1][:, :],
                             ab.t[:, base + 1024 + d * 128:base + 1024 + (d + 1) * 128],
                             ab.t[:, self.o_xn + d * L + hcols(t2).start:
                                  self.o_xn + d * L + hcols(t2).stop],
                             start=(d == 0), stop=(d == 7), skip_group_check=True,
                             inc=[("t", 1)] if d == 7 else ())
                    t_h = c.snap("t")
                    gsl = ab.t[:, self.o_w1sl + 4352:self.o_w1sl + 4864]
                    c.wait("s", "t", t_h)
                    c.wait("s", "v", self._act_v)
                    c.op("s", "activation", gsl, pb[pp + 1][:, :], AF.Silu, inc=[("s", 1)])
                    ssn = c.snap("s")
                    c.wait("v", "s", ssn)
                    act = ab.t[:, self.o_act + jp * 1024 + t2 * 512:
                               self.o_act + jp * 1024 + (t2 + 1) * 512]
                    c.wait("v", "t", self._w2_rd_t)
                    c.op("v", "tensor_tensor", act, pb[pp][:, :], gsl, ALU.mult,
                         inc=[("v", 1)])
                    self._act_v = c.snap("v")
                    self._h_bank_v[slot] = self._act_v
                self._w1_rd_t[slot] = c.snap("t")
            act_done_v = self._act_v
            c.wait("t", "g2", ld_w2)
            c.wait("t", "v", act_done_v)
            for t2 in range(2):
                for cc in range(8):
                    bank = 4 + (cc % 2)
                    c.wait("t", "v", self._eo_bank_v[cc % 2])
                    for f in range(16):
                        c.op("t", "matmul", pb[bank][:, :],
                             ab.t[:, self.o_w2b + f * 1024 + cc * 128:
                                  self.o_w2b + f * 1024 + (cc + 1) * 128],
                             ab.t[:, self.o_act + f * 1024 + t2 * 512:
                                  self.o_act + f * 1024 + (t2 + 1) * 512],
                             start=(f == 0), stop=(f == 15), skip_group_check=True,
                             inc=[("t", 1)] if f == 15 else ())
                    t_eo = c.snap("t")
                    c.wait("v", "t", t_eo)
                    c.wait("v", "v", acc0_v)
                    c.wait("v", "v", rwf_v)
                    eos = af.t[:, self.o_eos:self.o_eos + 512]
                    c.op("v", "tensor_tensor", eos, pb[bank][:, :],
                         af.t[:, self.o_rwf + t2 * 512:self.o_rwf + (t2 + 1) * 512],
                         ALU.mult)
                    accsl = af.t[:, self.o_acc + cc * 1024 + t2 * 512:
                                 self.o_acc + cc * 1024 + (t2 + 1) * 512]
                    c.op("v", "tensor_tensor", accsl, accsl, eos, ALU.add,
                         inc=[("v", 1)])
                    self._eo_bank_v[cc % 2] = c.snap("v")
            self._w2_rd_t = c.snap("t")
        for cc in range(8):
            xrb = af.t[:, self.o_xr2:self.o_xr2 + 1024]
            c.wait("q", "v", self._xr2_v)
            c.op("q", "dma_start", xrb,
                 T["xT"][cc * 128:(cc + 1) * 128, half * 1024:(half + 1) * 1024],
                 inc=[("qx0", 16)])
            ldx = c.snap("qx0")
            accs = af.t[:, self.o_acc + cc * 1024:self.o_acc + (cc + 1) * 1024]
            c.wait("v", "qx0", ldx)
            c.op("v", "tensor_tensor", accs, accs, xrb, ALU.add, inc=[("v", 1)])
            self._xr2_v = c.snap("v")
            c.op("v", "tensor_tensor", accs, accs,
                 ab.t[:, self.o_ao + cc * L + half * 1024:self.o_ao + cc * L + (half + 1) * 1024],
                 ALU.add, inc=[("v", 1)])
            vsn = c.snap("v")
            c.wait("q", "v", vsn)
            c.op("q", "dma_start",
                 self.y[cc * 128:(cc + 1) * 128, half * 1024:(half + 1) * 1024],
                 accs, inc=[("oq", 16)])
        self._y_oq = c.snap("oq")

    def ffn(self):
        a0 = self._dw_bank_v[0]
        t0 = self.kb.snap("t")
        v0 = self.kb.snap("v")
        self._h_bank_v = [a0, a0]
        self._eo_bank_v = [a0, a0]
        self._act_v = 0
        self._w1_rd_t = [t0, t0]
        self._w2_rd_t = t0
        self._xr2_v = v0
        self._y_oq = 0
        self.router()
        for half in range(2):
            self.ffn_half(half)
        self.kb.wait("q", "oq", self.kb.snap("oq"))

    def emit_mm0(self):
        c = self.kb
        ab2, af2, pb2 = self.ab, self.af, self.pb
        c.wait("t", "gw", self.ld_g)
        c.wait("t", "q", self.ld_q)
        if self.stage == "mm0f":
            c.op("t", "matmul", pb2[3][:, :], self.ones1,
                 af2.t[0:1, self.o_sm:self.o_sm + 512], start=True, stop=True,
                 inc=[("t", 1)])
            ones128 = af2.t[:, self.o_ones:self.o_ones + 1]
            for chk in range(8):
                c.op("t", "matmul", pb2[0][0:1, :], ones128,
                     af2.t[:, self.o_sm + 0:self.o_sm + 512],
                     start=(chk == 0), stop=(chk == 7),
                     inc=[("t", 1)] if chk == 7 else ())
        c.op("t", "matmul", pb2[5][0:16, :],
             ab2.t[:, self.o_wbeta:self.o_wbeta + 16],
             ab2.t[:, self.o_wv:self.o_wv + 512],
             start=True, stop=True, inc=[("t", 1)])
        c.wait("v", "t", c.snap("t"))
        c.op("v", "tensor_copy", af2.t[0:16, self.o_ost:self.o_ost + 512],
             pb2[5][0:16, :], inc=[("v", 1)])
        c.wait("q", "v", c.snap("v"))
        c.op("q", "dma_start", self.dbg["mm1"], af2.t[0:16, self.o_ost:self.o_ost + 512],
             inc=[("oq", 16)])
        c.wait("g", "gw", self.ld_g)
        c.op("g", "dma_start", self.dbg["wdump"], ab2.t[:, self.o_wbeta:self.o_wbeta + 16],
             inc=[("og", 16)])
        c.op("g", "dma_start", self.dbg["xdump"], ab2.t[:, self.o_wv:self.o_wv + 512],
             inc=[("og", 16)])
        c.wait("q", "oq", c.snap("oq"))
        self.store_y_stub()

    def dbg_store_xn(self):
        c, ab = self.kb, self.ab
        c.wait("g", "v", self.xn_done_v)
        for chk in range(8):
            c.op("g", "dma_start", self.dbg["xn"][chk * 128:(chk + 1) * 128, :],
                 ab.t[:, self.o_xn + chk * L:self.o_xn + (chk + 1) * L], inc=[("og", 16)])
        c.wait("g", "og", c.snap("og"))

    def store_y_stub(self):
        c, af = self.kb, self.af
        for chk in range(8):
            c.op("q", "dma_start", self.y[chk * 128:(chk + 1) * 128, :],
                 af.t[:, self.o_xr:self.o_xr + L], inc=[("oq", 16)])
        c.wait("q", "oq", c.snap("oq"))

    def emit(self):
        self._pp_evac_s = 0
        self._pp_conv_v = 0
        self._pp_red_t = 0
        self._pp_norm_v = 0
        self._pp_mul_v = 0
        self._pp_sil_s = 0
        self._ring_free_t = [0, 0]
        self._bns_free_t = 0
        self._dw_bank_v = [0] * 8
        self._last_proj_t = 0
        self._o_done_v = 0
        self._ao_done_s = 0
        self.const_loads()
        if self.stage in ("mm0", "mm0f"):
            self.emit_mm0()
            return
        self.layernorm()
        if self.stage == "xn":
            self.dbg_store_xn()
            self.store_y_stub()
            return
        if self.stage == "mm":
            c = self.kb
            ab2, af2, pb2 = self.ab, self.af, self.pb
            c.wait("t", "v", self.xn_done_v)
            c.wait("t", "gw", self.ld_g)
            c.op("t", "matmul", pb2[5][0:16, :],
                 ab2.t[:, self.o_wbeta:self.o_wbeta + 16],
                 ab2.t[:, self.o_xn:self.o_xn + 512],
                 start=True, stop=True, inc=[("t", 1)])
            c.wait("v", "t", c.snap("t"))
            c.op("v", "tensor_copy", af2.t[0:16, self.o_ost:self.o_ost + 512],
                 pb2[5][0:16, :], inc=[("v", 1)])
            c.wait("q", "v", c.snap("v"))
            c.op("q", "dma_start", self.dbg["mm1"], af2.t[0:16, self.o_ost:self.o_ost + 512],
                 inc=[("oq", 16)])
            c.wait("g", "v", self.xn_done_v)
            c.op("g", "dma_start", self.dbg["wdump"], ab2.t[:, self.o_wbeta:self.o_wbeta + 16],
                 inc=[("og", 16)])
            c.op("g", "dma_start", self.dbg["xdump"], ab2.t[:, self.o_xn:self.o_xn + 512],
                 inc=[("og", 16)])
            c.wait("q", "oq", c.snap("oq"))
            self.store_y_stub()
            return
        self.coef_build()
        self.beta_proj()
        if self.stage == "beta":
            c = self.kb
            ab2, af2, pb2 = self.ab, self.af, self.pb
            c.wait("t", "v", self.xn_done_v)
            c.wait("t", "gw", self.ld_g)
            c.op("t", "matmul", pb2[5][0:16, :],
                 ab2.t[:, self.o_wbeta:self.o_wbeta + 16],
                 ab2.t[:, self.o_xn:self.o_xn + 512],
                 start=True, stop=True, inc=[("t", 1)])
            tsn1 = c.snap("t")
            c.wait("v", "t", tsn1)
            c.op("v", "tensor_copy", af2.t[0:16, self.o_ost:self.o_ost + 512],
                 pb2[5][0:16, :], inc=[("v", 1)])
            c.wait("q", "v", c.snap("v"))
            c.op("q", "dma_start", self.dbg["mm1"], af2.t[0:16, self.o_ost:self.o_ost + 512],
                 inc=[("oq", 16)])
            c.wait("q", "s", self.beta_done_s)
            c.op("q", "dma_start", self.dbg["beta"], self.af.t[0:16, self.o_beta:self.o_beta + L],
                 inc=[("oq", 16)])
            c.wait("q", "v", c.snap("v"))
            c.op("q", "dma_start", self.dbg["lg"], self.af.t[0:16, self.o_xr:self.o_xr + L],
                 inc=[("oq", 16)])
            c.op("g", "dma_start", self.dbg["wb"], self.ab.t[:, self.o_wbeta:self.o_wbeta + 128],
                 inc=[("og", 16)])
            c.wait("q", "oq", c.snap("oq"))
            self.store_y_stub()
            return
        if self.stage == "pre":
            for b in range(NBLKP):
                self.preproc_block(b)
                self.dbg_store_pre(b)
                self.kb.wait("v", "oq", self._dbg_pre_o)
            self.store_y_stub()
            return
        self.delta_init()
        for b in range(NBLKP):
            self.preproc_block(b)
            self.delta_chunk(2 * b)
            self.delta_chunk(2 * b + 1)
        if self.stage in ("delta", "delta_dbg"):
            c = self.kb
            c.wait("g", "v", self._o_done_v)
            c.op("g", "dma_start", self.dbg["coef"],
                 self.af.t[:, self.o_coef:self.o_coef + 6 * NCH], inc=[("og", 16)])
            c.op("g", "dma_start", self.dbg["tmt"],
                 self.af.t[:, self.o_tmT:self.o_tmT + NCH * 64], inc=[("og", 16)])
            for p in range(8):
                c.op("g", "dma_start", self.dbg["o"][p * 128:(p + 1) * 128, :],
                     self.ab.t[:, self.o_ao + p * L:self.o_ao + (p + 1) * L],
                     inc=[("og", 16)])
            c.wait("g", "og", c.snap("og"))
            self.store_y_stub()
            return
        self.attn_proj()
        if self.stage in ("full", "ffn", "fulldbg"):
            self.ffn()
            if self.stage == "fulldbg":
                c = self.kb
                c.wait("g", "oq", c.snap("oq"))
                for nm, off, n, ar in (("w2b", self.o_w2b, 2048, "b"), ("slab", self.o_w1sl + 256, 2048, "b"),
                                       ("act", self.o_act, 2048, "b"), ("wrb", self.o_w1sl, 64, "b")):
                    c.op("g", "dma_start", self.dbg["f_" + nm],
                         self.ab.t[:, off:off + n], inc=[("og", 16)])
                c.op("q", "dma_start", self.dbg["f_rwfm"],
                     self.af.t[0:8, self.o_rwfm:self.o_rwfm + 2048], inc=[("oq", 16)])
                c.op("q", "dma_start", self.dbg["f_lgt"],
                     self.af.t[:, self.o_lgt:self.o_lgt + 128], inc=[("oq", 16)])
                c.op("q", "dma_start", self.dbg["f_topk"],
                     self.af.t[:, self.o_lgt + 128:self.o_lgt + 592], inc=[("oq", 16)])
                c.op("g", "dma_start", self.dbg["f_rwtm"],
                     self.ab.t[:, self.o_w1sl + 64:self.o_w1sl + 192], inc=[("og", 16)])
                c.wait("g", "og", c.snap("og"))
                c.wait("q", "oq", c.snap("oq"))
            return
        if self.stage == "attn":
            c = self.kb
            c.wait("g", "s", self._ao_done_s)
            for p in range(8):
                c.op("g", "dma_start", self.dbg["attn"][p * 128:(p + 1) * 128, :],
                     self.ab.t[:, self.o_ao + p * L:self.o_ao + (p + 1) * L],
                     inc=[("og", 16)])
            c.wait("g", "og", c.snap("og"))
            self.store_y_stub()
            return
        self.store_y_stub()


def build(stage):
    nc = bass.Bass("TRN2", target_bir_lowering=False, detect_race_conditions=False)
    kb = KB(nc)
    DT = {}

    def din(name, shape):
        DT[name] = nc.dram_tensor(name, list(shape), F32, kind="ExternalInput").ap()

    din("xT", (DM, L))
    for nm in ("wk", "wq", "wv", "wout"):
        din(nm, (128, 8192))
    for nm in ("bk", "bq", "bv", "bout"):
        din(nm, (128, 8))
    din("wbeta", (128, 128)); din("bbeta", (16, 1))
    for nm in ("convk", "convq", "convv"):
        din(nm, (128, 32))
    din("rmsw", (1, 512)); din("wr", (128, 64))
    din("w1s", (E * 32 * 128, 1024)); din("w2s", (E * 16 * 128, 1024))
    din("e8", (8, 1024))
    din("ident", (128, 128)); din("ones128", (128, 1)); din("ones1", (1, 128))
    din("epsv", (128, 5))
    din("mskl", (128, 64)); din("msku", (128, 64))
    din("e16", (16, 1024)); din("e16n", (16, 1024)); din("h16", (128, 128))
    for nm in ("a_num", "a_den", "wi_num", "wi_den", "c1_num", "c1_den", "c2_num", "c2_den"):
        din(nm, (128, NCH))
    din("tnumT", (128, NCH * 64)); din("tden", (1, NCH * 64))

    y = nc.dram_tensor("y", [DM, L], F32, kind="ExternalOutput").ap()
    dbg = {}
    if stage == "xn":
        dbg["xn"] = nc.dram_tensor("d_xn", [DM, L], F32, kind="ExternalOutput").ap()
    if stage == "mm":
        dbg["mm1"] = nc.dram_tensor("d_mm1", [16, 512], F32, kind="ExternalOutput").ap()
        dbg["wdump"] = nc.dram_tensor("d_wdump", [128, 16], F32, kind="ExternalOutput").ap()
        dbg["xdump"] = nc.dram_tensor("d_xdump", [128, 512], F32, kind="ExternalOutput").ap()
    if stage in ("mm0", "mm0f"):
        dbg["mm1"] = nc.dram_tensor("d_mm1", [16, 512], F32, kind="ExternalOutput").ap()
        dbg["wdump"] = nc.dram_tensor("d_wdump", [128, 16], F32, kind="ExternalOutput").ap()
        dbg["xdump"] = nc.dram_tensor("d_xdump", [128, 512], F32, kind="ExternalOutput").ap()
    if stage == "fulldbg":
        for nm, n in (("w2b", 2048), ("slab", 2048), ("act", 2048), ("wrb", 64)):
            dbg["f_" + nm] = nc.dram_tensor("f_" + nm, [128, n], F32, kind="ExternalOutput").ap()
        dbg["f_rwfm"] = nc.dram_tensor("f_rwfm", [8, 2048], F32, kind="ExternalOutput").ap()
        dbg["f_lgt"] = nc.dram_tensor("f_lgt", [128, 128], F32, kind="ExternalOutput").ap()
        dbg["f_topk"] = nc.dram_tensor("f_topk", [128, 464], F32, kind="ExternalOutput").ap()
        dbg["f_rwtm"] = nc.dram_tensor("f_rwtm", [128, 128], F32, kind="ExternalOutput").ap()
    if stage in ("delta", "delta_dbg"):
        dbg["o"] = nc.dram_tensor("d_o", [DM, L], F32, kind="ExternalOutput").ap()
        dbg["coef"] = nc.dram_tensor("d_coef", [128, 6 * NCH], F32, kind="ExternalOutput").ap()
        dbg["coef2"] = nc.dram_tensor("d_coef2", [128, 6 * NCH], F32, kind="ExternalOutput").ap()
        dbg["cdbg"] = nc.dram_tensor("d_cdbg", [128, 3 * NCH], F32, kind="ExternalOutput").ap()
        dbg["tmt"] = nc.dram_tensor("d_tmt", [128, NCH * 64], F32, kind="ExternalOutput").ap()
        dbg["z_ost"] = nc.dram_tensor("dz_ost", [128, 1024], F32, kind="ExternalOutput").ap()
        dbg["z_rmswb"] = nc.dram_tensor("dz_rmswb", [128, 512], F32, kind="ExternalOutput").ap()
        dbg["z_oc0"] = nc.dram_tensor("dz_oc0", [128, 512], F32, kind="ExternalOutput").ap()
        dbg["z_onb"] = nc.dram_tensor("dz_onb", [128, 512], F32, kind="ExternalOutput").ap()
        dbg["z_qwmu"] = nc.dram_tensor("dz_qwmu", [128, 1536], F32, kind="ExternalOutput").ap()
        dbg["z_ssq"] = nc.dram_tensor("dz_ssq", [128, 8], F32, kind="ExternalOutput").ap()
        for nm, n in (("mt", 512), ("pg", 1024), ("usb", 512), ("ktm", 512),
                      ("gfm", 512), ("mpf", 512), ("mgt", 512), ("x0t", 512), ("on", 512)):
            dbg["z_" + nm] = nc.dram_tensor("dz_" + nm, [128, n], F32, kind="ExternalOutput").ap()
    if stage == "attn":
        dbg["attn"] = nc.dram_tensor("d_attn", [DM, L], F32, kind="ExternalOutput").ap()
    if stage == "beta":
        dbg["beta"] = nc.dram_tensor("d_beta", [16, L], F32, kind="ExternalOutput").ap()
        dbg["lg"] = nc.dram_tensor("d_lg", [16, L], F32, kind="ExternalOutput").ap()
        dbg["wb"] = nc.dram_tensor("d_wb", [128, 128], F32, kind="ExternalOutput").ap()
        dbg["wb2"] = nc.dram_tensor("d_wb2", [128, 128], F32, kind="ExternalOutput").ap()
        dbg["mm1"] = nc.dram_tensor("d_mm1", [16, 512], F32, kind="ExternalOutput").ap()
    if stage == "pre":
        for nm in ("kh", "qh", "vh", "ck", "cq", "cv"):
            dbg[nm] = nc.dram_tensor("d_" + nm, [DM, L], F32, kind="ExternalOutput").ap()
        dbg["rn"] = nc.dram_tensor("d_rn", [16, 256 * NBLKP], F32, kind="ExternalOutput").ap()
        dbg["ns"] = nc.dram_tensor("d_ns", [16, 256], F32, kind="ExternalOutput").ap()
        dbg["beta"] = nc.dram_tensor("d_beta", [16, L], F32, kind="ExternalOutput").ap()

    with ExitStack() as ctx:
        af = Arena(ctx.enter_context(nc.sbuf_tensor("af32", [128, 16128], F32)), 16128)
        ab = Arena(ctx.enter_context(nc.sbuf_tensor("abf16", [128, 72448], BF16)), 72448)
        pbs = [ctx.enter_context(nc.psum_tensor(f"pb{i}", [128, 512], F32)) for i in range(8)]
        kb.mksem(ctx, "q", "g", "t", "v", "s", "o", "qc", "qx0", "qx1", "qk", "gw", "gwo", "g1a", "g1b", "g2", "og", "oq", "qc2")
        import os as _os
        kb.strict = bool(int(_os.environ.get("KSTRICT", "0")))
        block = ctx.enter_context(nc.Block())
        P = Prog(nc, kb, DT, y, dbg, af, ab, pbs, stage)

        @block.sync
        def _(eng):
            kb.begin("q", eng); P.emit()

        @block.gpsimd
        def _(eng):
            kb.begin("g", eng); P.emit()

        @block.tensor
        def _(eng):
            kb.begin("t", eng); P.emit()

        @block.vector
        def _(eng):
            kb.begin("v", eng); P.emit()

        @block.scalar
        def _(eng):
            kb.begin("s", eng); P.emit()

    return nc




def build_full():
    return build("full")


_NC_CACHE = {}


def kernel(**inputs):
    x = np.asarray(inputs["L0_attn_x"])
    if "full" not in _NC_CACHE:
        _NC_CACHE["full"] = build_full()
    nc = _NC_CACHE["full"]
    base = host_inputs(x[0], inputs)
    in_maps = []
    for core in range(8):
        m = dict(base)
        m["xT"] = np.ascontiguousarray(np.asarray(x[core], np.float32).T)
        in_maps.append(m)
    res = run_bass_kernel_spmd(nc, in_maps, core_ids=list(range(8)))
    out = np.stack([np.asarray(r["y"]).T for r in res.results], axis=0)
    return out.astype(np.float32)
